# version 13
# baseline (speedup 1.0000x reference)
import numpy as np
import ml_dtypes
from contextlib import ExitStack
import concourse.bass as bass
import concourse.mybir as mybir
from concourse.bass_utils import run_bass_kernel_spmd

F32 = mybir.dt.float32
BF16 = mybir.dt.bfloat16
AF = mybir.ActivationFunctionType
ALU = mybir.AluOpType

D = 1024
NB = 2
S = 8192
NCORE = 8
PC = S // NCORE
TOK = NB * PC
KG = D // 128
NCH = 4
CW = TOK // NCH
NS = NB * KG

ENGINES = ("sync", "gpsimd", "tensor", "scalar", "vector")
DEBUG_B = False


DMA_POOL = 6


class _Op:
    __slots__ = ("eng", "fn", "reads", "writes", "dma", "force", "deps",
                 "sig", "val", "waits", "sem_key", "chain")

    def __init__(self, eng, fn, reads, writes, dma, force):
        self.eng, self.fn = eng, fn
        self.reads, self.writes = list(reads), list(writes)
        self.dma, self.force = dma, force
        self.deps = set()
        self.sig = False
        self.val = None
        self.waits = []
        self.sem_key = eng
        self.chain = None


class Builder:

    def __init__(self):
        self.ops = []
        self.res = {}
        self.qn = {}
        self._qlist = {}

    def op(self, eng, fn, reads=(), writes=(), dma=False, force=False):
        i = len(self.ops)
        o = _Op(eng, fn, reads, writes, dma, force)
        if dma:
            q = self.qn.get(eng, 0)
            o.sem_key = f"{eng}_d{q % DMA_POOL}"
            if q >= DMA_POOL:
                o.chain = self._qlist[eng][q - DMA_POOL]
            self._qlist.setdefault(eng, []).append(i)
            self.qn[eng] = q + 1
        for k in o.reads:
            e = self.res.get(k)
            if e:
                o.deps.update(e["w"])
        for k in o.writes:
            e = self.res.get(k)
            if e:
                o.deps.update(e["w"])
                o.deps.update(e["r"].values())
        for k in o.reads:
            self.res.setdefault(k, {"w": [], "r": {}})["r"][eng] = i
        for k in o.writes:
            self.res.setdefault(k, {"w": [], "r": {}})["w"].append(i)
        o.deps.discard(i)
        self.ops.append(o)
        return i

    def finalize(self, block, semh):
        ops = self.ops
        for i, o in enumerate(ops):
            kept = {}
            for d in o.deps:
                p = ops[d]
                if p.eng == o.eng:
                    if p.eng == "tensor":
                        continue
                    if p.dma and o.dma:
                        continue
                if kept.get(p.sem_key, -1) < d:
                    kept[p.sem_key] = d
            if o.chain is not None:
                ck = ops[o.chain].sem_key
                if kept.get(ck, -1) < o.chain:
                    kept[ck] = o.chain
            o.deps = set(kept.values())
        for o in ops:
            for d in o.deps:
                ops[d].sig = True
        for o in ops:
            if o.force or o.dma:
                o.sig = True
        cnt = {}
        for o in ops:
            if o.sig:
                cnt[o.sem_key] = cnt.get(o.sem_key, 0) + (16 if o.dma else 1)
                o.val = cnt[o.sem_key]
        for o in ops:
            w = {}
            for d in o.deps:
                p = ops[d]
                w[p.sem_key] = max(w.get(p.sem_key, 0), p.val)
            o.waits = sorted(w.items())
        per_eng = {e: [o for o in ops if o.eng == e] for e in ENGINES}
        final_pool = {e: sorted({o.sem_key: o.val for o in per_eng[e]
                                 if o.dma and o.sig}.items())
                      for e in ENGINES}
        final_pool = {}
        for e in ENGINES:
            vals = {}
            for o in per_eng[e]:
                if o.dma and o.sig:
                    vals[o.sem_key] = max(vals.get(o.sem_key, 0), o.val)
            final_pool[e] = sorted(vals.items())

        def make_body(ename):
            def body(eng):
                clock = {}
                for o in per_eng[ename]:
                    for sname, val in o.waits:
                        if clock.get(sname, 0) < val:
                            eng.wait_ge(semh[sname], val)
                            clock[sname] = val
                    inst = o.fn(eng)
                    if o.sig:
                        inst.then_inc(semh[o.sem_key], 16 if o.dma else 1)
                for sname, val in final_pool[ename]:
                    if clock.get(sname, 0) < val:
                        eng.wait_ge(semh[sname], val)
                        clock[sname] = val
            return body

        for e in ENGINES:
            getattr(block, e)(make_body(e))



def build_A():
    nc = bass.Bass("TRN2", target_bir_lowering=False, debug=False)
    xT = nc.dram_tensor("xT", [D, TOK], F32, kind="ExternalInput").ap()
    wsrc = [nc.dram_tensor(nm, [D, D], F32, kind="ExternalInput").ap()
            for nm in ("wz_f", "wh_f", "wz_b", "wh_b")]
    bcols = nc.dram_tensor("bcols", [128, 4 * KG], F32, kind="ExternalInput").ap()
    o_xnT = nc.dram_tensor("o_xnT", [D, TOK], BF16, kind="ExternalOutput").ap()
    o_pl = {}
    for d in ("f", "b"):
        o_pl[("h", d)] = nc.dram_tensor(f"o_h{d}", [D, TOK], BF16, kind="ExternalOutput").ap()
        o_pl[("p", d)] = nc.dram_tensor(f"o_p{d}", [D, TOK], BF16, kind="ExternalOutput").ap()

    B = Builder()
    with ExitStack() as st:
        ec = st.enter_context
        xnT = ec(nc.sbuf_tensor("xnT", [128, KG * TOK], BF16))
        bc = ec(nc.sbuf_tensor("bc", [128, 4 * KG], F32))
        bzneg = ec(nc.sbuf_tensor("bzneg", [128, 2 * KG], F32))
        bh05 = ec(nc.sbuf_tensor("bh05", [128, 2 * KG], F32))
        onesc = ec(nc.sbuf_tensor("onesc", [128, 1], F32))
        ones1 = ec(nc.sbuf_tensor("ones1", [1, 128], F32))
        wmat = ec(nc.sbuf_tensor("wmat", [128, 4 * KG * D], BF16))
        psk = ec(nc.psum_tensor("psk", [128, TOK], F32))
        psu = ec(nc.psum_tensor("psu", [128, TOK], F32))
        semh = {"sync": ec(nc.semaphore("s_sync")), "gpsimd": ec(nc.semaphore("s_gp")),
                "tensor": ec(nc.semaphore("s_pe")), "scalar": ec(nc.semaphore("s_act")),
                "vector": ec(nc.semaphore("s_ve"))}
        for q in ("sync", "gpsimd"):
            for r in range(DMA_POOL):
                semh[f"{q}_d{r}"] = ec(nc.semaphore(f"s_{q}_d{r}"))
        block = ec(nc.Block())

        def row(i, tc=None):
            if tc is None:
                return rows[0:1, i * TOK:(i + 1) * TOK]
            return rows[0:1, i * TOK + tc * CW: i * TOK + (tc + 1) * CW]

        def wtile(wi, kg, m):
            return wmat[:, (wi * KG + kg) * D + m * 128:
                        (wi * KG + kg) * D + (m + 1) * 128]

        ph1 = ExitStack()
        xfull = ph1.enter_context(nc.sbuf_tensor("xfull", [128, KG * TOK], F32))
        scr = ph1.enter_context(nc.sbuf_tensor("scr", [128, 3 * CW], F32))
        rows = ph1.enter_context(nc.sbuf_tensor("rows", [1, 4 * TOK], F32))
        if True:
            for kg in range(KG):
                B.op("sync", (lambda kg: lambda e: e.dma_start(
                    out=xfull[:, kg * TOK:(kg + 1) * TOK],
                    in_=xT[kg * 128:(kg + 1) * 128, :]))(kg),
                    writes=[f"xf{kg}"], dma=True)
            B.op("sync", lambda e: e.dma_start(out=bc[:], in_=bcols[:, :]),
                 writes=["bc"], dma=True)
            for wi in range(4):
                for kg in range(KG):
                    B.op("gpsimd", (lambda wi, kg: lambda e: e.dma_start(
                        out=wmat[:, (wi * KG + kg) * D:(wi * KG + kg + 1) * D],
                        in_=wsrc[wi][kg * 128:(kg + 1) * 128, :]))(wi, kg),
                        writes=[f"w{wi}"], dma=True)
            B.op("gpsimd", lambda e: e.memset(onesc[:], 1.0), writes=["onesc"])
            B.op("gpsimd", lambda e: e.memset(ones1[:], 1.0), writes=["ones1"])

            ps_s1 = psu[0:1, 0:CW]
            ps_s2 = psu[0:1, CW:2 * CW]
            for tc in range(NCH):
                for kg in range(KG):
                    sp = scr[:, (kg % 2) * CW:(kg % 2 + 1) * CW]
                    B.op("scalar", (lambda kg, tc, sp: lambda e: e.activation(
                        sp, xfull[:, kg * TOK + tc * CW: kg * TOK + (tc + 1) * CW],
                        AF.Square))(kg, tc, sp),
                        reads=[f"xf{kg}"], writes=[f"scr{kg % 2}"])
                    B.op("tensor", (lambda kg, tc: lambda e: e.matmul(
                        ps_s1, onesc[:, 0:1],
                        xfull[:, kg * TOK + tc * CW: kg * TOK + (tc + 1) * CW],
                        start=(kg == 0), stop=(kg == KG - 1)))(kg, tc),
                        reads=[f"xf{kg}", "onesc"], writes=["psu_s"])
                    B.op("tensor", (lambda kg, sp: lambda e: e.matmul(
                        ps_s2, onesc[:, 0:1], sp,
                        start=(kg == 0), stop=(kg == KG - 1)))(kg, sp),
                        reads=[f"scr{kg % 2}", "onesc"], writes=["psu_s"])
                B.op("vector", (lambda tc: lambda e: e.tensor_copy(
                    row(0, tc), ps_s1))(tc), reads=["psu_s"], writes=["row0"])
                B.op("vector", (lambda tc: lambda e: e.tensor_copy(
                    row(1, tc), ps_s2))(tc), reads=["psu_s"], writes=["row1"])
            B.op("vector", lambda e: e.tensor_scalar_mul(row(0), row(0), 1.0 / D),
                 reads=["row0"], writes=["row0"])
            B.op("vector", lambda e: e.tensor_mul(row(2), row(0), row(0)),
                 reads=["row0"], writes=["row2"])
            B.op("vector", lambda e: e.scalar_tensor_tensor(
                row(3), row(1), 1.0 / D, row(2), op0=ALU.mult, op1=ALU.subtract),
                reads=["row1", "row2"], writes=["row3"])
            B.op("vector", lambda e: e.tensor_scalar_add(row(3), row(3), 1e-5),
                 reads=["row3"], writes=["row3"])
            B.op("vector", lambda e: e.reciprocal(row(3), row(3)),
                 reads=["row3"], writes=["row3"])
            B.op("scalar", lambda e: e.activation(row(3), row(3), AF.Sqrt),
                 reads=["row3"], writes=["row3"])
            B.op("vector", lambda e: e.scalar_tensor_tensor(
                row(2), row(0), -1.0, row(3), op0=ALU.mult, op1=ALU.mult),
                reads=["row0", "row3"], writes=["row2b"])
            B.op("vector", lambda e: e.tensor_scalar_mul(bzneg[:, 0:KG], bc[:, 0:KG], -1.0),
                 reads=["bc"], writes=["bzneg"])
            B.op("vector", lambda e: e.tensor_scalar_mul(
                bzneg[:, KG:2 * KG], bc[:, 2 * KG:3 * KG], -1.0),
                reads=["bc"], writes=["bzneg"])
            B.op("vector", lambda e: e.tensor_scalar_add(bh05[:, 0:KG], bc[:, KG:2 * KG], 0.5),
                 reads=["bc"], writes=["bh05"])
            B.op("vector", lambda e: e.tensor_scalar_add(
                bh05[:, KG:2 * KG], bc[:, 3 * KG:4 * KG], 0.5),
                reads=["bc"], writes=["bh05"])

            psr_b = psk[:, 0:CW]
            psm_b = psk[:, CW:2 * CW]
            sapp = scr[:, 2 * CW:3 * CW]
            for tc in range(NCH):
                B.op("tensor", (lambda tc: lambda e: e.matmul(
                    psr_b, ones1[0:1, :], row(3, tc), start=True, stop=True))(tc),
                    reads=["row3", "ones1"], writes=["psk_b"])
                B.op("tensor", (lambda tc: lambda e: e.matmul(
                    psm_b, ones1[0:1, :], row(2, tc), start=True, stop=True))(tc),
                    reads=["row2b", "ones1"], writes=["psk_b"])
                for kg in range(KG):
                    co = kg * TOK + tc * CW
                    B.op("vector", (lambda co: lambda e: e.tensor_mul(
                        sapp, xfull[:, co:co + CW], psr_b))(co),
                        reads=[f"xf{kg}", "psk_b"], writes=["sapp"])
                    B.op("vector", (lambda co: lambda e: e.tensor_add(
                        xnT[:, co:co + CW], sapp, psm_b))(co),
                        reads=["sapp", "psk_b"], writes=[f"xn{kg}t{tc}"])

        ph1.close()

        for kg in range(KG):
            B.op("sync", (lambda kg: lambda e: e.dma_start(
                out=o_xnT[kg * 128:(kg + 1) * 128, :],
                in_=xnT[:, kg * TOK:(kg + 1) * TOK]))(kg),
                reads=[f"xn{kg}t{tc}" for tc in range(NCH)], dma=True, force=True)

        zrs = ec(nc.sbuf_tensor("zrs", [128, PC], F32))
        B.op("gpsimd", lambda e: e.memset(zrs[:], 0.0),
             reads=[f"xn{kg}t{tc}" for kg in range(KG) for tc in range(NCH)]
             + ["psk_b", "row3"],
             writes=["zrs"])
        abuf = ec(nc.sbuf_tensor("abuf", [128, 2 * TOK], F32))
        bbuf = ec(nc.sbuf_tensor("bbuf", [128, 2 * TOK], F32))
        zbuf = ec(nc.sbuf_tensor("zbuf", [128, TOK], F32))
        sgbuf = ec(nc.sbuf_tensor("sgbuf", [128, TOK], F32))
        tbuf = ec(nc.sbuf_tensor("tbuf", [128, TOK], F32))
        hsc = ec(nc.sbuf_tensor("hsc", [128, 2 * TOK], F32))
        psc = ec(nc.sbuf_tensor("psc", [128, 2 * TOK], F32))
        if True:
            for di, dnm in enumerate(("f", "b")):
                for m in range(KG):
                    it = di * KG + m
                    par = it % 2
                    for wi, psq, pk in ((2 * di, psk, "psk"), (2 * di + 1, psu, "psu")):
                        extra = ["psk_b"] if pk == "psk" else ["psu_s"]
                        for tc2 in range(NCH):
                            for kg in range(KG):
                                rds = ([f"w{wi}"] + [f"xn{k}t{tc2}" for k in range(KG)]
                                       if kg == 0 else [])
                                B.op("tensor", (lambda wi, kg, m, tc2, psq: lambda e: e.matmul(
                                    psq[:, tc2 * CW:(tc2 + 1) * CW],
                                    wtile(wi, kg, m),
                                    xnT[:, kg * TOK + tc2 * CW: kg * TOK + (tc2 + 1) * CW],
                                    start=(kg == 0), stop=(kg == KG - 1)))(wi, kg, m, tc2, psq),
                                    reads=rds,
                                    writes=[pk] + (extra if kg == 0 and tc2 == 0 else []))
                    ab = abuf[:, par * TOK:(par + 1) * TOK]
                    bb = bbuf[:, par * TOK:(par + 1) * TOK]
                    hs = hsc[:, par * TOK:(par + 1) * TOK]
                    pscm = psc[:, par * TOK:(par + 1) * TOK]
                    bzn = bzneg[:, di * KG + m: di * KG + m + 1]
                    bzp = bc[:, 2 * di * KG + m: 2 * di * KG + m + 1]
                    bhp = bc[:, (2 * di + 1) * KG + m: (2 * di + 1) * KG + m + 1]
                    bh5 = bh05[:, di * KG + m: di * KG + m + 1]
                    B.op("scalar", (lambda ab, bzn: lambda e: e.activation(
                        ab, psk[:, :], AF.Sigmoid, bias=bzn, scale=-1.0))(ab, bzn),
                        reads=["psk", "bzneg"], writes=[f"ab{par}"])
                    B.op("scalar", (lambda bzp: lambda e: e.activation(
                        zbuf[:], psk[:, :], AF.Sigmoid, bias=bzp))(bzp),
                        reads=["psk", "bc"], writes=["zbuf"])
                    B.op("scalar", (lambda bhp: lambda e: e.activation(
                        sgbuf[:], psu[:, :], AF.Sigmoid, bias=bhp))(bhp),
                        reads=["psu", "bc"], writes=["sgbuf"])
                    B.op("vector", (lambda bh5: lambda e: e.scalar_tensor_tensor(
                        tbuf[:], psu[:, :], bh5, sgbuf[:],
                        op0=ALU.add, op1=ALU.max))(bh5),
                        reads=["psu", "bh05", "sgbuf"], writes=["tbuf"])
                    B.op("vector", (lambda bb: lambda e: e.tensor_mul(
                        bb, zbuf[:], tbuf[:]))(bb),
                        reads=["zbuf", "tbuf"], writes=[f"bb{par}"])
                    for b in range(NB):
                        cs = slice(b * PC, (b + 1) * PC)
                        if dnm == "f":
                            a_ap, b_ap = ab[:, cs], bb[:, cs]
                        else:
                            a_ap, b_ap = ab[:, cs][:, ::-1], bb[:, cs][:, ::-1]
                        B.op("vector", (lambda hs, cs, a_ap, b_ap: lambda e: e.tensor_tensor_scan(
                            hs[:, cs], a_ap, b_ap, 0.0, op0=ALU.mult, op1=ALU.add))(hs, cs, a_ap, b_ap),
                            reads=[f"ab{par}", f"bb{par}"], writes=[f"hs{par}"])
                        B.op("vector", (lambda pscm, cs, a_ap: lambda e: e.tensor_tensor_scan(
                            pscm[:, cs], a_ap, zrs[:, :], 1.0,
                            op0=ALU.mult, op1=ALU.add))(pscm, cs, a_ap),
                            reads=[f"ab{par}", "zrs"], writes=[f"ps{par}"])
                    B.op("gpsimd", (lambda dnm, m, hs: lambda e: e.dma_start(
                        out=o_pl[("h", dnm)][m * 128:(m + 1) * 128, :], in_=hs))(dnm, m, hs),
                        reads=[f"hs{par}"], dma=True, force=True)
                    B.op("gpsimd", (lambda dnm, m, pscm: lambda e: e.dma_start(
                        out=o_pl[("p", dnm)][m * 128:(m + 1) * 128, :], in_=pscm))(dnm, m, pscm),
                        reads=[f"ps{par}"], dma=True, force=True)

            B.finalize(block, semh)
    return nc



def build_B():
    nc = bass.Bass("TRN2", target_bir_lowering=False, debug=False)
    xT = nc.dram_tensor("xT", [D, TOK], F32, kind="ExternalInput").ap()
    i_xnT = nc.dram_tensor("i_xnT", [D, TOK], BF16, kind="ExternalInput").ap()
    ipl = {nm: nc.dram_tensor(nm, [D, TOK], BF16, kind="ExternalInput").ap()
           for nm in ("i_hf", "i_pf", "i_hb", "i_pb")}
    sumf = nc.dram_tensor("sumf", [128, NCORE * 2 * NS], BF16, kind="ExternalInput").ap()
    sumb = nc.dram_tensor("sumb", [128, NCORE * 2 * NS], BF16, kind="ExternalInput").ap()
    w_out = nc.dram_tensor("w_out", [2 * D, D], F32, kind="ExternalInput").ap()
    fw = {nm: nc.dram_tensor(nm, [D, D], F32, kind="ExternalInput").ap()
          for nm in ("wfT", "ufT", "wiT", "uiT", "wcT", "ucT", "woT", "uoT")}
    bcols = nc.dram_tensor("bcols", [128, 6 * KG], F32, kind="ExternalInput").ap()
    o_out = nc.dram_tensor("o_out", [D, TOK], F32, kind="ExternalOutput").ap()
    if DEBUG_B:
        d_s = nc.dram_tensor("d_s", [D, TOK], F32, kind="ExternalOutput").ap()
        d_g = nc.dram_tensor("d_g", [D, TOK], F32, kind="ExternalOutput").ap()
        d_hfc = nc.dram_tensor("d_hfc", [D, TOK], BF16, kind="ExternalOutput").ap()
        d_hbc = nc.dram_tensor("d_hbc", [D, TOK], BF16, kind="ExternalOutput").ap()
        d_cbf = nc.dram_tensor("d_cbf", [D, PC], BF16, kind="ExternalOutput").ap()
        d_pref = nc.dram_tensor("d_pref", [128, 18 * NS], F32, kind="ExternalOutput").ap()

    B = Builder()
    PAIRS = [("f", 0), ("i", 0), ("c", 0), ("f", 1), ("i", 1), ("c", 1), ("o", None)]
    WMAT = {"f": ("wfT", "ufT"), "i": ("wiT", "uiT"), "c": ("wcT", "ucT"),
            "o": ("woT", "uoT")}

    with ExitStack() as st:
        ec = st.enter_context
        xnT = ec(nc.sbuf_tensor("xnT", [128, KG * TOK], BF16))
        bcs = ec(nc.sbuf_tensor("bcs", [128, 6 * KG], F32))
        cbf = ec(nc.sbuf_tensor("cbf", [128, KG * PC], BF16))
        sums = ec(nc.sbuf_tensor("sums", [128, 2 * NCORE * 2 * NS], BF16))
        pref = ec(nc.sbuf_tensor("pref", [128, 18 * NS], F32))
        wqA = ec(nc.sbuf_tensor("wqA", [128, 2 * KG * D], BF16))
        psg = ec(nc.psum_tensor("psg", [128, 4 * PC], F32))
        semh = {"sync": ec(nc.semaphore("s_sync")), "gpsimd": ec(nc.semaphore("s_gp")),
                "tensor": ec(nc.semaphore("s_pe")), "scalar": ec(nc.semaphore("s_act")),
                "vector": ec(nc.semaphore("s_ve"))}
        for q in ("sync", "gpsimd"):
            for r in range(DMA_POOL):
                semh[f"{q}_d{r}"] = ec(nc.semaphore(f"s_{q}_d{r}"))
        block = ec(nc.Block())
        WQ = [wqA, None]

        def load_pair(li):
            q, _t = PAIRS[li]
            wn, un = WMAT[q]
            buf = WQ[li % 2]
            for half, nm in ((0, wn), (1, un)):
                for kg in range(KG):
                    B.op("gpsimd", (lambda buf, half, kg, nm: lambda e: e.dma_start(
                        out=buf[:, (half * KG + kg) * D:(half * KG + kg + 1) * D],
                        in_=fw[nm][kg * 128:(kg + 1) * 128, :]))(buf, half, kg, nm),
                        writes=[f"wqb{li % 2}"], dma=True)

        def ptile(li, kg, half, m):
            buf = WQ[li % 2]
            off = (half * KG + kg) * D
            return buf[:, off + m * 128: off + (m + 1) * 128]

        for kg in range(KG):
            B.op("sync", (lambda kg: lambda e: e.dma_start(
                out=xnT[:, kg * TOK:(kg + 1) * TOK],
                in_=i_xnT[kg * 128:(kg + 1) * 128, :]))(kg),
                writes=[f"xn{kg}"], dma=True)
        B.op("sync", lambda e: e.dma_start(out=bcs[:], in_=bcols[:, :]),
             writes=["bcs"], dma=True)
        B.op("sync", lambda e: e.dma_start(out=sums[:, 0:NCORE * 2 * NS], in_=sumf[:, :]),
             writes=["sums"], dma=True)
        B.op("sync", lambda e: e.dma_start(out=sums[:, NCORE * 2 * NS:], in_=sumb[:, :]),
             writes=["sums"], dma=True)
        load_pair(0)

        B.op("vector", lambda e: e.memset(pref[:, 0:NS], 0.0), writes=["pref0"])
        B.op("vector", lambda e: e.memset(pref[:, 9 * NS:10 * NS], 0.0), writes=["pref9"])
        for half in range(2):
            sbase = half * NCORE * 2 * NS
            pb0 = half * 9
            for r in range(NCORE):
                hcol = sbase + (r * 2 + 0) * NS
                pcol = sbase + (r * 2 + 1) * NS
                src = pref[:, (pb0 + r) * NS: (pb0 + r + 1) * NS]
                dst = pref[:, (pb0 + r + 1) * NS: (pb0 + r + 2) * NS]
                B.op("vector", (lambda dst, pcol, src: lambda e: e.tensor_mul(
                    dst, sums[:, pcol:pcol + NS], src))(dst, pcol, src),
                    reads=["sums", f"pref{pb0 + r}"], writes=[f"pref{pb0 + r + 1}"])
                B.op("vector", (lambda dst, hcol: lambda e: e.tensor_add(
                    dst, dst, sums[:, hcol:hcol + NS]))(dst, hcol),
                    reads=["sums", f"pref{pb0 + r + 1}"], writes=[f"pref{pb0 + r + 1}"])

        xnreads = [f"xn{kg}" for kg in range(KG)]
        cbreads = [f"cb{kg}" for kg in range(KG)]

        hfc = ec(nc.sbuf_tensor("hfc", [128, 2 * KG * CW], BF16))
        hbc = ec(nc.sbuf_tensor("hbc", [128, 2 * KG * CW], BF16))
        stg = ec(nc.sbuf_tensor("stg", [128, 4 * CW], BF16))

        ph2 = ExitStack()
        cst = ph2.enter_context(nc.sbuf_tensor("cst", [128, KG * PC], F32))
        gpl = ph2.enter_context(nc.sbuf_tensor("gpl", [128, KG * PC], BF16))
        gcm = ph2.enter_context(nc.sbuf_tensor("gcm", [128, PC], F32))
        tmpg = ph2.enter_context(nc.sbuf_tensor("tmpg", [128, PC], F32))
        wqB = ph2.enter_context(nc.sbuf_tensor("wqB", [128, 2 * KG * D], BF16))
        if True:
            WQ[1] = wqB
            load_pair(1)
            for m in range(KG):
                icol = bcs[:, 5 * KG + m: 5 * KG + m + 1]
                B.op("vector", (lambda m, icol: lambda e: e.tensor_scalar(
                    cst[:, m * PC:(m + 1) * PC], xnT[:, 0:PC], 0.0, icol,
                    op0=ALU.mult, op1=ALU.add))(m, icol),
                    reads=["bcs", "xn0"], writes=[f"c{m}"])
                B.op("vector", (lambda m, icol: lambda e: e.tensor_scalar(
                    cbf[:, m * PC:(m + 1) * PC], xnT[:, 0:PC], 0.0, icol,
                    op0=ALU.mult, op1=ALU.add))(m, icol),
                    reads=["bcs", "xn0"], writes=[f"cb{m}"])

            for t in range(2):
                for qi, q in enumerate(("f", "i", "c")):
                    li = t * 3 + qi
                    for m in range(KG):
                        slot = m % 4
                        for nh in range(2):
                            c0 = slot * PC + nh * CW
                            bk = [f"pb{c0 // CW}"]
                            for kg in range(KG):
                                rds = ([f"wqb{li % 2}"] + xnreads + cbreads
                                       if kg == 0 else [])
                                B.op("tensor", (lambda li, kg, m, t, nh, c0: lambda e: e.matmul(
                                    psg[:, c0:c0 + CW],
                                    ptile(li, kg, 0, m),
                                    xnT[:, kg * TOK + t * PC + nh * CW: kg * TOK + t * PC + (nh + 1) * CW],
                                    start=(kg == 0), stop=False))(li, kg, m, t, nh, c0),
                                    reads=rds, writes=bk)
                            for kg in range(KG):
                                B.op("tensor", (lambda li, kg, m, nh, c0: lambda e: e.matmul(
                                    psg[:, c0:c0 + CW],
                                    ptile(li, kg, 1, m),
                                    cbf[:, kg * PC + nh * CW: kg * PC + (nh + 1) * CW],
                                    start=False, stop=(kg == KG - 1)))(li, kg, m, nh, c0),
                                    reads=(cbreads if kg == KG - 1 else []), writes=bk)
                        fn = AF.Tanh if q == "c" else AF.Sigmoid
                        bq = bcs[:, qi * KG + m: qi * KG + m + 1]
                        pbk = [f"pb{2 * slot}", f"pb{2 * slot + 1}"]
                        if q != "c":
                            B.op("scalar", (lambda m, slot, fn, bq: lambda e: e.activation(
                                gpl[:, m * PC:(m + 1) * PC],
                                psg[:, slot * PC:(slot + 1) * PC], fn, bias=bq))(m, slot, fn, bq),
                                reads=pbk + ["bcs"], writes=[f"gpl{m}"])
                        else:
                            B.op("scalar", (lambda m, slot, fn, bq: lambda e: e.activation(
                                gcm[:, :], psg[:, slot * PC:(slot + 1) * PC],
                                fn, bias=bq))(m, slot, fn, bq),
                                reads=pbk + ["bcs"], writes=["gcm"])
                            B.op("vector", (lambda m: lambda e: e.tensor_mul(
                                tmpg[:, :], gpl[:, m * PC:(m + 1) * PC],
                                gcm[:, :]))(m),
                                reads=[f"gpl{m}", "gcm"], writes=["tmpg"])
                            B.op("vector", (lambda m: lambda e: e.tensor_add(
                                cst[:, m * PC:(m + 1) * PC], cst[:, m * PC:(m + 1) * PC],
                                tmpg[:, :]))(m),
                                reads=["tmpg", f"c{m}"], writes=[f"c{m}"])
                    if q == "f":
                        for m in range(KG):
                            B.op("vector", (lambda m: lambda e: e.tensor_mul(
                                cst[:, m * PC:(m + 1) * PC], cst[:, m * PC:(m + 1) * PC],
                                gpl[:, m * PC:(m + 1) * PC]))(m),
                                reads=[f"gpl{m}", f"c{m}"], writes=[f"c{m}"])
                    if li + 2 <= 6:
                        load_pair(li + 2)
                for m in range(KG):
                    B.op("scalar", (lambda m: lambda e: e.activation(
                        cbf[:, m * PC:(m + 1) * PC], cst[:, m * PC:(m + 1) * PC],
                        AF.Copy))(m),
                        reads=[f"c{m}"], writes=[f"cb{m}"])

            ffn_keys = ([f"gpl{m}" for m in range(KG)] + [f"c{m}" for m in range(KG)]
                        + [f"cb{m}" for m in range(KG)] + ["gcm", "tmpg"])
            B.op("vector", lambda e: e.engine_nop(), reads=ffn_keys,
                 writes=["ffn_done", "wqb1"])
            if DEBUG_B:
                B.op("sync", lambda e: e.dma_start(out=d_pref[:, :], in_=pref[:, :]),
                     reads=[f"pref{i}" for i in range(18)], dma=True, force=True)
                for kg in range(KG):
                    B.op("sync", (lambda kg: lambda e: e.dma_start(
                        out=d_cbf[kg * 128:(kg + 1) * 128, :],
                        in_=cbf[:, kg * PC:(kg + 1) * PC]))(kg),
                        reads=[f"cb{kg}"], dma=True, force=True)

        ph2.close()
        wout_s = ec(nc.sbuf_tensor("wout_s", [128, 2 * KG * D], BF16))
        xch = ec(nc.sbuf_tensor("xch", [128, 2 * CW], F32))
        sch = ec(nc.sbuf_tensor("sch", [128, KG * CW], F32))
        uoc = ec(nc.sbuf_tensor("uoc", [128, KG * CW], F32))
        hf2 = ec(nc.sbuf_tensor("hf2", [128, 2 * CW], F32))
        sg2 = ec(nc.sbuf_tensor("sg2", [128, 2 * CW], F32))
        g2 = ec(nc.sbuf_tensor("g2", [128, 2 * CW], F32))
        p2 = ec(nc.sbuf_tensor("p2", [128, 2 * CW], F32))
        och = ec(nc.sbuf_tensor("och", [128, 2 * CW], F32))
        if True:
            FD = ["ffn_done"]
            for kg in range(2 * KG):
                B.op("gpsimd", (lambda kg: lambda e: e.dma_start(
                    out=wout_s[:, kg * D:(kg + 1) * D],
                    in_=w_out[kg * 128:(kg + 1) * 128, :]))(kg),
                    reads=FD, writes=["wout"], dma=True)

            TCS = (0, 2, 1, 3)

            def emit_chunk_loads(tci):
                tc = TCS[tci]
                par = tci % 2
                b = tc // 2
                for dirn, (hnm, pnm, dst, dkey, prefkey, prefblk) in enumerate((
                        ("i_hf", "i_pf", hfc, "hfc", "pref8", 8),
                        ("i_hb", "i_pb", hbc, "hbc", "pref17", 17))):
                    B.op("sync", (lambda hnm, tc, par, dst: lambda e: e.dma_start(
                        out=dst.ap().rearrange("p (q kg c) -> p q kg c", q=2, c=CW)[:, par],
                        in_=ipl[hnm][:, tc * CW:(tc + 1) * CW].rearrange(
                            "(kg p) c -> p kg c", p=128)))(hnm, tc, par, dst),
                        writes=[f"{dkey}L{par}"], dma=True)
                    for kh in range(2):
                        B.op("sync", (lambda pnm, kh, tc: lambda e: e.dma_start(
                            out=stg.ap().rearrange("p (kg c) -> p kg c", c=CW),
                            in_=ipl[pnm][kh * 512:(kh + 1) * 512, tc * CW:(tc + 1) * CW].rearrange(
                                "(kg p) c -> p kg c", p=128)))(pnm, kh, tc),
                            writes=["stg"], dma=True)
                        for kg in range(kh * 4, kh * 4 + 4):
                            pcol = pref[:, prefblk * NS + b * KG + kg: prefblk * NS + b * KG + kg + 1]
                            B.op("vector", (lambda dst, par, kg, pcol: lambda e: e.scalar_tensor_tensor(
                                dst[:, par * KG * CW + kg * CW: par * KG * CW + (kg + 1) * CW],
                                stg[:, (kg % 4) * CW:(kg % 4 + 1) * CW], pcol,
                                dst[:, par * KG * CW + kg * CW: par * KG * CW + (kg + 1) * CW],
                                op0=ALU.mult, op1=ALU.add))(dst, par, kg, pcol),
                                reads=["stg", f"{dkey}L{par}", prefkey],
                                writes=[f"{dkey}{par}"])
            emit_chunk_loads(0)
            for tci, tc in enumerate(TCS):
                par = tci % 2
                b = tc // 2
                ppos = tc - 2 * b
                if tci + 1 < NCH:
                    emit_chunk_loads(tci + 1)
                if tci % 2 == 0:
                    for m in range(KG):
                        slot = m % 4
                        fc0 = 4 * CW + slot * CW
                        for kg in range(KG):
                            rds = ["wqb0"] + cbreads if kg == 0 else []
                            B.op("tensor", (lambda kg, m, ppos, fc0: lambda e: e.matmul(
                                psg[:, fc0:fc0 + CW],
                                ptile(6, kg, 1, m),
                                cbf[:, kg * PC + ppos * CW: kg * PC + (ppos + 1) * CW],
                                start=(kg == 0), stop=(kg == KG - 1)))(kg, m, ppos, fc0),
                                reads=rds, writes=[f"pb{4 + slot}"])
                        B.op("scalar", (lambda m, fc0: lambda e: e.activation(
                            uoc[:, m * CW:(m + 1) * CW], psg[:, fc0:fc0 + CW],
                            AF.Copy))(m, fc0),
                            reads=FD + [f"pb{4 + (m % 4)}"], writes=[f"uoc{m}"])
                for m in range(KG):
                    slot = m % 4
                    for kgc in range(2 * KG):
                        if kgc < KG:
                            mv = hfc[:, par * KG * CW + kgc * CW: par * KG * CW + (kgc + 1) * CW]
                        else:
                            mv = hbc[:, par * KG * CW + (kgc - KG) * CW: par * KG * CW + (kgc - KG + 1) * CW]
                        rds = (["wout", f"hfc{par}", f"hbc{par}"]
                               if kgc in (0, 2 * KG - 1) else [])
                        B.op("tensor", (lambda kgc, m, slot, mv: lambda e: e.matmul(
                            psg[:, slot * CW:(slot + 1) * CW],
                            wout_s[:, kgc * D + m * 128: kgc * D + (m + 1) * 128],
                            mv, start=(kgc == 0), stop=(kgc == 2 * KG - 1)))(kgc, m, slot, mv),
                            reads=rds, writes=[f"pb{slot}"])
                    bocol = bcs[:, 4 * KG + m: 4 * KG + m + 1]
                    B.op("vector", (lambda m, slot, bocol: lambda e: e.tensor_scalar(
                        sch[:, m * CW:(m + 1) * CW], psg[:, slot * CW:(slot + 1) * CW],
                        bocol, None, op0=ALU.add))(m, slot, bocol),
                        reads=FD + [f"pb{slot}", "bcs"], writes=[f"sch{m}"])
                for m in range(KG):
                    slot = m % 4
                    mp = m % 2
                    fc0 = 4 * CW + slot * CW
                    B.op("sync", (lambda m, tc, mp: lambda e: e.dma_start(
                        out=xch[:, mp * CW:(mp + 1) * CW],
                        in_=xT[m * 128:(m + 1) * 128, tc * CW:(tc + 1) * CW]))(m, tc, mp),
                        reads=FD, writes=[f"xch{mp}"], dma=True)
                    for kg in range(KG):
                        rds = ["wqb0"] + xnreads if kg == 0 else []
                        B.op("tensor", (lambda kg, m, tc, fc0: lambda e: e.matmul(
                            psg[:, fc0:fc0 + CW],
                            ptile(6, kg, 0, m),
                            xnT[:, kg * TOK + tc * CW: kg * TOK + (tc + 1) * CW],
                            start=(kg == 0), stop=(kg == KG - 1)))(kg, m, tc, fc0),
                            reads=rds, writes=[f"pb{4 + slot}"])
                    B.op("vector", (lambda mp, fc0, m: lambda e: e.tensor_add(
                        hf2[:, mp * CW:(mp + 1) * CW], psg[:, fc0:fc0 + CW],
                        uoc[:, m * CW:(m + 1) * CW]))(mp, fc0, m),
                        reads=FD + [f"pb{4 + slot}", f"uoc{m}"], writes=[f"hf2{mp}"])
                    bfcol = bcs[:, 3 * KG + m: 3 * KG + m + 1]
                    B.op("scalar", (lambda mp, bfcol: lambda e: e.activation(
                        hf2[:, mp * CW:(mp + 1) * CW], hf2[:, mp * CW:(mp + 1) * CW],
                        AF.Tanh, bias=bfcol))(mp, bfcol),
                        reads=[f"hf2{mp}", "bcs"], writes=[f"hf2{mp}"])
                    B.op("scalar", (lambda mp: lambda e: e.activation(
                        sg2[:, mp * CW:(mp + 1) * CW], hf2[:, mp * CW:(mp + 1) * CW],
                        AF.Sigmoid))(mp),
                        reads=FD + [f"hf2{mp}"], writes=[f"sg2{mp}"])
                    B.op("vector", (lambda mp: lambda e: e.tensor_mul(
                        g2[:, mp * CW:(mp + 1) * CW], hf2[:, mp * CW:(mp + 1) * CW],
                        sg2[:, mp * CW:(mp + 1) * CW]))(mp),
                        reads=FD + [f"hf2{mp}", f"sg2{mp}"], writes=[f"g2{mp}"])
                    B.op("vector", (lambda m, mp: lambda e: e.tensor_mul(
                        p2[:, mp * CW:(mp + 1) * CW], sch[:, m * CW:(m + 1) * CW],
                        g2[:, mp * CW:(mp + 1) * CW]))(m, mp),
                        reads=FD + [f"g2{mp}", f"sch{m}"], writes=[f"p2{mp}"])
                    oq = m % 2
                    B.op("vector", (lambda m, mp, oq: lambda e: e.tensor_add(
                        och[:, oq * CW:(oq + 1) * CW], p2[:, mp * CW:(mp + 1) * CW],
                        xch[:, mp * CW:(mp + 1) * CW]))(m, mp, oq),
                        reads=FD + [f"p2{mp}", f"xch{mp}"], writes=[f"och{oq}"])
                    B.op("sync", (lambda m, tc, oq: lambda e: e.dma_start(
                        out=o_out[m * 128:(m + 1) * 128, tc * CW:(tc + 1) * CW],
                        in_=och[:, oq * CW:(oq + 1) * CW]))(m, tc, oq),
                        reads=[f"och{oq}"], dma=True, force=True)

            B.finalize(block, semh)
    return nc



def _bias_cols(*vecs):
    cols = [np.ascontiguousarray(np.asarray(v, np.float32).reshape(KG, 128).T)
            for v in vecs]
    return np.ascontiguousarray(np.concatenate(cols, axis=1))


def kernel(**inputs):
    f32 = np.float32
    x = np.asarray(inputs["L0_attn_x"], f32)
    g = lambda k: np.asarray(inputs[k], f32)

    xT_shards = []
    for c in range(NCORE):
        sl = x[:, c * PC:(c + 1) * PC, :]
        xT_shards.append(np.ascontiguousarray(sl.transpose(2, 0, 1).reshape(D, TOK)))

    bcols_A = _bias_cols(g("L0_attn_bz_f"), g("L0_attn_bh_f"),
                         g("L0_attn_bz_b"), g("L0_attn_bh_b"))
    wA = {nm: np.ascontiguousarray(g(src)) for nm, src in
          (("wz_f", "L0_attn_Wz_f"), ("wh_f", "L0_attn_Wh_f"),
           ("wz_b", "L0_attn_Wz_b"), ("wh_b", "L0_attn_Wh_b"))}

    ncA = build_A()
    in_maps_A = [dict(xT=xT_shards[c], bcols=bcols_A, **wA) for c in range(NCORE)]
    resA = run_bass_kernel_spmd(ncA, in_maps_A, core_ids=list(range(NCORE)))
    A = resA.results

    bf = ml_dtypes.bfloat16
    hf = [np.asarray(A[c]["o_hf"]).reshape(KG, 128, TOK) for c in range(NCORE)]
    pf = [np.asarray(A[c]["o_pf"]).reshape(KG, 128, TOK) for c in range(NCORE)]
    hb = [np.asarray(A[c]["o_hb"]).reshape(KG, 128, TOK) for c in range(NCORE)]
    pb = [np.asarray(A[c]["o_pb"]).reshape(KG, 128, TOK) for c in range(NCORE)]

    def _ends(plane_r):
        e = plane_r[:, :, [PC - 1, TOK - 1]]
        return np.ascontiguousarray(e.transpose(1, 2, 0).reshape(128, NS))

    zero = np.zeros((128, NS), bf)
    one = np.ones((128, NS), bf)
    sumf_arr, sumb_arr = [], []
    for c in range(NCORE):
        bl_f, bl_b = [], []
        for r in range(NCORE):
            if r < c:
                bl_f += [_ends(hf[r]), _ends(pf[r])]
                owner = NCORE - 1 - r
                bl_b += [_ends(hb[owner]), _ends(pb[owner])]
            else:
                bl_f += [zero, one]
                bl_b += [zero, one]
        sumf_arr.append(np.ascontiguousarray(np.concatenate(bl_f, axis=1)))
        sumb_arr.append(np.ascontiguousarray(np.concatenate(bl_b, axis=1)))

    bcols_B = _bias_cols(g("L0_ffn_bf"), g("L0_ffn_bi"), g("L0_ffn_bc"),
                         g("L0_ffn_bo"), g("L0_attn_b_out"), g("L0_ffn_in_c"))
    w_out = np.ascontiguousarray(g("L0_attn_W_out"))
    fwt = {nm + "T": np.ascontiguousarray(g("L0_ffn_" + nm).T)
           for nm in ("wf", "uf", "wi", "ui", "wc", "uc", "wo", "uo")}

    ncB = build_B()
    in_maps_B = []
    for c in range(NCORE):
        partner = NCORE - 1 - c
        in_maps_B.append(dict(
            xT=xT_shards[c],
            i_xnT=np.asarray(A[c]["o_xnT"]),
            i_hf=np.asarray(A[c]["o_hf"]),
            i_pf=np.asarray(A[c]["o_pf"]),
            i_hb=np.asarray(A[partner]["o_hb"]),
            i_pb=np.asarray(A[partner]["o_pb"]),
            sumf=sumf_arr[c], sumb=sumb_arr[c],
            w_out=w_out, bcols=bcols_B, **fwt))
    resB = run_bass_kernel_spmd(ncB, in_maps_B, core_ids=list(range(NCORE)))

    out = np.empty((NB, S, D), f32)
    for c in range(NCORE):
        oc = np.asarray(resB.results[c]["o_out"])
        out[:, c * PC:(c + 1) * PC, :] = oc.reshape(D, NB, PC).transpose(1, 2, 0)
    return out
